# version 01
# baseline (speedup 1.0000x reference)
import numpy as np
import ml_dtypes
import concourse.bass as bass
import concourse.mybir as mybir
from concourse.alu_op_type import AluOpType
from concourse.bass_utils import run_bass_kernel_spmd

F32 = mybir.dt.float32
BF16 = mybir.dt.bfloat16
FP16 = mybir.dt.float16
F8 = mybir.dt.float8e4
I32 = mybir.dt.int32
AF = mybir.ActivationFunctionType
AX = mybir.AxisListType
OP = AluOpType

B, T, D, HID = 4, 8192, 1024, 4096
GR = 512
NGR = 8
NJ = 4
TOK = NGR * GR
EPS = 1e-5



class Buf:

    __slots__ = ("ap", "writer", "readers", "name")

    def __init__(self, ap, name=""):
        self.ap = ap
        self.writer = None
        self.readers = []
        self.name = name

    @property
    def shape(self):
        return self.ap.shape


class P:

    ENGS = ("tensor", "vector", "scalar", "gpsimd", "sync")

    def __init__(self, nc):
        self.nc = nc
        self.ops = {k: [] for k in self.ENGS}
        self.cnt = {}
        self.sems = {}

    def _key(self, key):
        if key not in self.sems:
            self.sems[key] = self.nc.alloc_semaphore(name=f"s_{key}")
            self.cnt[key] = 0
        return key

    def _deps(self, reads, writes, deps):
        dep = [d for d in deps if d]
        for b in reads:
            if isinstance(b, Buf) and b.writer:
                dep.append(b.writer)
        for b in writes:
            if isinstance(b, Buf):
                if b.writer:
                    dep.append(b.writer)
                dep.extend(b.readers)
        return dep

    def _mark(self, tok, reads, writes):
        for b in reads:
            if isinstance(b, Buf):
                b.readers.append(tok)
        for b in writes:
            if isinstance(b, Buf):
                b.writer = tok
                b.readers = []

    def op(self, eng, fn, reads=(), writes=(), deps=(), inc=True):
        key = self._key(eng)
        dep = self._deps(reads, writes, deps)
        tok = None
        if inc:
            self.cnt[key] += 1
            tok = (key, self.cnt[key])
        self.ops[eng].append((fn, tuple(sorted(set(dep))),
                              key if inc else None, 1))
        if tok:
            self._mark(tok, reads, writes)
        return tok

    NSLOT = 8

    def dma(self, queue, fn, reads=(), writes=(), deps=()):
        sidx = self.cnt.get("_slot_" + queue, 0)
        self.cnt["_slot_" + queue] = (sidx + 1) % self.NSLOT
        key = self._key(f"dma_{queue}_{sidx}")
        dep = self._deps(reads, writes, deps)
        if self.cnt[key]:
            dep.append((key, self.cnt[key]))
        self.cnt[key] += 16
        tok = (key, self.cnt[key])
        self.ops[queue].append((fn, tuple(sorted(set(dep))), key, 16))
        self._mark(tok, reads, writes)
        return tok

    def mm_group(self, fns, reads=(), write=None, deps=()):
        key = self._key("tensor")
        dep = self._deps(reads, (write,) if write else (), deps)
        n = len(fns)
        for i, fn in enumerate(fns):
            last = i == n - 1
            if last:
                self.cnt[key] += 1
                tok = (key, self.cnt[key])
            self.ops["tensor"].append(
                (fn, tuple(sorted(set(dep))) if i == 0 else (),
                 key if last else None, 1))
        self._mark(tok, reads, (write,) if write else ())
        return tok

    def barrier(self):
        alldeps = tuple((k, v) for k, v in self.cnt.items()
                        if not k.startswith("_slot_") and v > 0)
        for eng in self.ENGS:
            self.op(eng, lambda e: e.nop(), deps=alldeps, inc=False)

    def emit(self):
        nc = self.nc
        with nc.Block() as block:
            def runner(name):
                def _(eng):
                    waited = {}
                    for fn, deps, inckey, amt in self.ops[name]:
                        for (pkey, pval) in deps:
                            if waited.get(pkey, 0) >= pval:
                                continue
                            eng.wait_ge(self.sems[pkey], pval)
                            waited[pkey] = pval
                        inst = fn(eng)
                        if inckey is not None:
                            inst.then_inc(self.sems[inckey], amt)
                    if name == "gpsimd":
                        for key, sem in self.sems.items():
                            if key.startswith("dma_"):
                                eng.wait_ge(sem, self.cnt[key])
                return _
            block.sync(runner("sync"))
            block.scalar(runner("scalar"))
            block.vector(runner("vector"))
            block.tensor(runner("tensor"))
            block.gpsimd(runner("gpsimd"))
        for s in self.sems.values():
            self.nc.release_semaphore(s)


class Ctx:

    def __init__(self, nc, st):
        from contextlib import ExitStack
        self.nc = nc
        self.stack = [st]
        self.p = P(nc)
        self.PS = [Buf(st.enter_context(nc.psum_tensor(f"psA{i}", [128, 512],
                                                       F32)).ap(), f"psA{i}")
                   for i in range(6)]
        self.PT = Buf(st.enter_context(nc.psum_tensor("psT", [128, 512],
                                                      BF16)).ap(), "psT")
        self.PR = Buf(st.enter_context(nc.psum_tensor("psR", [128, 512],
                                                      F32)).ap(), "psR")

    def sb(self, name, shape, dt=F32):
        return Buf(self.stack[-1].enter_context(
            self.nc.sbuf_tensor(name, shape, dt)).ap(), name)

    def dr(self, name, shape, dt=F32):
        return Buf(self.nc.dram_tensor(name, shape, dt, kind="Internal").ap(),
                   name)

    def scope(self):
        from contextlib import ExitStack, contextmanager

        @contextmanager
        def _scope():
            st = ExitStack()
            self.stack.append(st)
            try:
                yield
            finally:
                self.p.barrier()
                self.stack.pop()
                st.close()
        return _scope()


def make_identity(c, name="ident", dt=BF16):
    ident = c.sb(name, [128, 128], dt)
    c.p.op("gpsimd", lambda e: e.memset(ident.ap[:], 0.0), writes=(ident,))
    c.p.op("gpsimd", lambda e: e.affine_select(
        out=ident.ap[:], in_=ident.ap[:], compare_op=OP.not_equal, fill=1.0,
        base=0, pattern=[[-1, 128]], channel_multiplier=1),
        reads=(ident,), writes=(ident,))
    return ident


def bcaster(c, ident_f):
    ones_row = c.sb("ones_row", [1, 128])
    c.p.op("gpsimd", lambda e: e.memset(ones_row.ap[:], 1.0),
           writes=(ones_row,))

    def to_col(dst, dst_sl, src, src_sl):
        c.p.op("tensor", lambda e: e.matmul(
            c.PR.ap[:, 0:1], ones_row.ap[0:1, :], src.ap[src_sl],
            start=True, stop=True), reads=(ones_row, src), writes=(c.PR,))
        return c.p.op("vector", lambda e: e.tensor_copy(
            dst.ap[dst_sl], c.PR.ap[:, 0:1]), reads=(c.PR,), writes=(dst,))

    def to_tile(dst, dst_sl, src, src_sl, n):
        assert n <= 512
        c.p.op("tensor", lambda e: e.matmul(
            c.PR.ap[:, 0:n], ones_row.ap[0:1, :], src.ap[src_sl],
            start=True, stop=True), reads=(ones_row, src), writes=(c.PR,))
        return c.p.op("vector", lambda e: e.tensor_copy(
            dst.ap[dst_sl], c.PR.ap[:, 0:n]), reads=(c.PR,), writes=(dst,))

    def col_to_row(dst, dst_sl, src, src_sl, n=128):
        c.p.op("tensor", lambda e: e.transpose(
            c.PR.ap[0:1, 0:128], src.ap[src_sl], ident_f.ap[:]),
            reads=(src, ident_f), writes=(c.PR,))
        return c.p.op("vector", lambda e: e.tensor_copy(
            dst.ap[dst_sl], c.PR.ap[0:1, 0:n]), reads=(c.PR,), writes=(dst,))

    return to_col, to_tile, col_to_row


def round_to(c, out, out_sl, x, x_sl, s_col, r_col, scr_i32, scr_f,
             s_is_tile=False):
    p = c.p
    if s_is_tile:
        p.op("vector", lambda e: e.tensor_tensor(
            scr_f.ap[:], x.ap[x_sl], s_col[0].ap[s_col[1]], op=OP.mult),
            reads=(x, s_col[0]), writes=(scr_f,))
        p.op("vector", lambda e: e.tensor_scalar(
            scr_i32.ap[:], scr_f.ap[:], 0.5, None, op0=OP.add),
            reads=(scr_f,), writes=(scr_i32,))
        p.op("vector", lambda e: e.scalar_tensor_tensor(
            scr_f.ap[:], scr_i32.ap[:], -0.5, scr_f.ap[:],
            op0=OP.add, op1=OP.is_gt), reads=(scr_i32, scr_f),
            writes=(scr_f,))
    else:
        p.op("vector", lambda e: e.tensor_scalar(
            scr_i32.ap[:], x.ap[x_sl], s_col[0].ap[s_col[1]], 0.5,
            op0=OP.mult, op1=OP.add), reads=(x, s_col[0]), writes=(scr_i32,))
        p.op("vector", lambda e: e.tensor_scalar(
            scr_f.ap[:], scr_i32.ap[:], -0.5, r_col[0].ap[r_col[1]],
            op0=OP.add, op1=OP.mult), reads=(scr_i32, r_col[0]),
            writes=(scr_f,))
        p.op("vector", lambda e: e.tensor_tensor(
            scr_f.ap[:], scr_f.ap[:], x.ap[x_sl], op=OP.is_gt),
            reads=(scr_f, x), writes=(scr_f,))
    return p.op("vector", lambda e: e.tensor_tensor(
        out.ap[out_sl], scr_i32.ap[:], scr_f.ap[:], op=OP.subtract),
        reads=(scr_i32, scr_f), writes=(out,))



def build_launch_A():
    from contextlib import ExitStack
    nc = bass.Bass("TRN2", target_bir_lowering=False)
    I = {}
    for nm, sh in [("xg", [TOK, D]), ("xh", [16, D]), ("wq", [D, D]),
                   ("wk", [D, D]), ("wv", [D, D]), ("bq_c", [128, 8]),
                   ("bk_c", [128, 8]), ("bv_r", [1, D]), ("w1t", [D, HID]),
                   ("w2t", [HID, D]), ("cw_c", [128, 32, 3]),
                   ("cb_c", [128, 32])]:
        I[nm] = Buf(nc.dram_tensor(nm, sh, F32, kind="ExternalInput").ap(), nm)

    def out_d(nm, sh, dt):
        return Buf(nc.dram_tensor(nm, sh, dt, kind="ExternalOutput").ap(), nm)

    o_q = out_d("o_q", [8, 128, TOK], BF16)
    o_k = out_d("o_k", [8, 128, TOK], BF16)
    o_v = out_d("o_v", [TOK, D], BF16)
    o_qn = out_d("o_qn", [1, TOK], F32)
    o_kx = out_d("o_kx", [1, 1], F32)
    o_g0 = out_d("o_g0", [TOK, D], F32)

    with ExitStack() as st:
        c = Ctx(nc, st)
        p = c.p
        ident = make_identity(c)
        ident_f = make_identity(c, "ident_f", F32)
        to_col, to_tile, col_to_row = bcaster(c, ident_f)
        ones_b = c.sb("ones_b", [128, 1], BF16)
        p.op("gpsimd", lambda e: e.memset(ones_b.ap[:], 1.0), writes=(ones_b,))
        xq_dram = c.dr("xq_scr", [8, 128, TOK], BF16)
        cmax_cols = c.sb("cmax_cols", [128, 32])
        cmax_hc = c.sb("cmax_hc", [128, 1])
        xq_hf = c.sb("xq_hf", [128, 8, 16], BF16)

        with c.scope():
            emit_A_phase1(c, I, o_q, o_k, o_v, o_qn, o_kx, xq_dram,
                          cmax_cols, cmax_hc, xq_hf, ident, ident_f,
                          to_col, to_tile, col_to_row, ones_b)
        emit_A_ffn(c, I, o_g0, xq_dram, xq_hf, cmax_cols, cmax_hc,
                   ident, ident_f, to_col, to_tile, col_to_row, ones_b)
        p.emit()
    return nc


def emit_A_phase1(c, I, o_q, o_k, o_v, o_qn, o_kx, xq_dram, cmax_cols,
                  cmax_hc, xq_hf, ident, ident_f, to_col, to_tile,
                  col_to_row, ones_b):
    p = c.p
    wq_t = c.sb("wq_t", [128, 8, D], BF16)
    wk_t = c.sb("wk_t", [128, 8, D], BF16)
    wv_t = c.sb("wv_t", [128, 8, D], BF16)
    for w_in, w_sb in [(I["wq"], wq_t), (I["wk"], wk_t), (I["wv"], wv_t)]:
        p.dma("gpsimd", lambda e, a=w_in, b=w_sb: e.dma_start(
            out=b.ap[:], in_=a.ap.rearrange("(c p) o -> p c o", p=128)),
            reads=(w_in,), writes=(w_sb,))
    bqc = c.sb("bqc", [128, 8])
    bkc = c.sb("bkc", [128, 8])
    bvr = c.sb("bvr", [1, D])
    p.dma("sync", lambda e: e.dma_start(out=bqc.ap[:], in_=I["bq_c"].ap),
          writes=(bqc,))
    p.dma("sync", lambda e: e.dma_start(out=bkc.ap[:], in_=I["bk_c"].ap),
          writes=(bkc,))
    p.dma("sync", lambda e: e.dma_start(out=bvr.ap[:], in_=I["bv_r"].ap),
          writes=(bvr,))
    p.op("vector", lambda e: e.tensor_scalar(
        bqc.ap[:], bqc.ap[:], 1.0 / 32.0, None, op0=OP.mult),
        reads=(bqc,), writes=(bqc,))
    bv_bc = c.sb("bv_bc", [128, D])
    to_tile(bv_bc, np.s_[:, 0:512], bvr, np.s_[0:1, 0:512], 512)
    to_tile(bv_bc, np.s_[:, 512:1024], bvr, np.s_[0:1, 512:1024], 512)

    eps_c = c.sb("eps_c", [128, 1])
    p.op("gpsimd", lambda e: e.memset(eps_c.ap[:], EPS), writes=(eps_c,))
    xt = [c.sb(f"xt{i}", [128, 4, D]) for i in range(2)]
    xnb = c.sb("xnb", [128, 4, D], BF16)
    stat6 = c.sb("stat6", [128, 12])
    mv = c.sb("mv", [128, 2])
    rstd = c.sb("rstd", [128, 1])
    nmb = c.sb("nmb", [128, 1])
    scol = c.sb("scol", [128, 4])
    rcol = c.sb("rcol", [128, 4])
    ci32 = c.sb("ci32", [128, D], I32)
    rscr = c.sb("rscr", [128, D])
    xqb = c.sb("xqb", [128, 4, D], BF16)
    xn32 = c.sb("xn32", [128, D])
    xqo = c.sb("xqo", [128, 8, GR], BF16)
    ko = c.sb("ko", [128, 8, GR], BF16)
    qo = c.sb("qo", [128, 8, GR], BF16)
    vo = c.sb("vo", [128, 4, D], BF16)
    xnf = c.sb("xnf", [128, 8, GR], BF16)
    sqt = c.sb("sqt", [128, 8, GR], BF16)
    nrm = c.sb("nrm", [1, GR])
    kxs = c.sb("kxs", [1, 2])

    def do_tile(ti):
        halo = ti == 8
        xtile = xt[ti % 2]
        col0 = ti * GR
        if halo:
            p.op("gpsimd", lambda e: e.memset(xtile.ap[:], 0.0),
                 writes=(xtile,))
            p.dma("sync", lambda e: e.dma_start(
                out=xtile.ap[0:16, 0, :], in_=I["xh"].ap),
                reads=(I["xh"], xtile), writes=(xtile,))
        else:
            p.dma("sync", lambda e, ti=ti: e.dma_start(
                out=xtile.ap[:],
                in_=I["xg"].ap.rearrange("(c p) d -> p c d", p=128)
                [:, 4 * ti:4 * ti + 4, :]), reads=(I["xg"],),
                writes=(xtile,))
        nch = 1 if halo else 4
        for ch in range(nch):
            p.op("vector", lambda e, ch=ch: e.bn_stats(
                stat6.ap[:, 0:6], xtile.ap[:, ch, 0:512]),
                reads=(xtile,), writes=(stat6,))
            p.op("vector", lambda e, ch=ch: e.bn_stats(
                stat6.ap[:, 6:12], xtile.ap[:, ch, 512:1024]),
                reads=(xtile,), writes=(stat6,))
            p.op("vector", lambda e: e.bn_aggr(mv.ap[:], stat6.ap[:]),
                 reads=(stat6,), writes=(mv,))
            p.op("scalar", lambda e: e.activation(
                rstd.ap[:], mv.ap[:, 1:2], AF.Sqrt, bias=eps_c.ap[:, 0:1]),
                reads=(mv, eps_c), writes=(rstd,))
            p.op("vector", lambda e: e.reciprocal(rstd.ap[:], rstd.ap[:]),
                 reads=(rstd,), writes=(rstd,))
            p.op("vector", lambda e: e.tensor_scalar(
                nmb.ap[:], mv.ap[:, 0:1], rstd.ap[:, 0:1], -1.0,
                op0=OP.mult, op1=OP.mult), reads=(mv, rstd), writes=(nmb,))
            p.op("scalar", lambda e, ch=ch: e.activation(
                xn32.ap[:], xtile.ap[:, ch, :], AF.Identity,
                bias=nmb.ap[:, 0:1], scale=rstd.ap[:, 0:1]),
                reads=(xtile, nmb, rstd), writes=(xn32,))
            p.op("scalar", lambda e, ch=ch: e.copy(
                xnb.ap[:, ch, :], xn32.ap[:]), reads=(xn32,), writes=(xnb,))
            cslot = (cmax_hc, np.s_[:, 0:1]) if halo else \
                (cmax_cols, np.s_[:, 4 * ti + ch:4 * ti + ch + 1])
            p.op("vector", lambda e, ch=ch, cs=cslot: e.tensor_reduce(
                cs[0].ap[cs[1]], xn32.ap[:], axis=AX.X,
                op=OP.max, apply_absolute_value=True),
                reads=(xn32,), writes=(cslot[0],))
            p.op("vector", lambda e, cs=cslot: e.tensor_scalar(
                cs[0].ap[cs[1]], cs[0].ap[cs[1]], 1e-5, None,
                op0=OP.max), reads=(cslot[0],), writes=(cslot[0],))
            p.op("vector", lambda e, ch=ch, cs=cslot: e.reciprocal(
                scol.ap[:, ch:ch + 1], cs[0].ap[cs[1]]),
                reads=(cslot[0],), writes=(scol,))
            p.op("vector", lambda e, ch=ch: e.tensor_scalar(
                scol.ap[:, ch:ch + 1], scol.ap[:, ch:ch + 1], 127.0, None,
                op0=OP.mult), reads=(scol,), writes=(scol,))
            p.op("vector", lambda e, ch=ch, cs=cslot: e.tensor_scalar(
                rcol.ap[:, ch:ch + 1], cs[0].ap[cs[1]], 1.0 / 127.0,
                None, op0=OP.mult), reads=(cslot[0],), writes=(rcol,))
            round_to(c, xqb, np.s_[:, ch, :], xn32, np.s_[:, :],
                     (scol, np.s_[:, ch:ch + 1]), (rcol, np.s_[:, ch:ch + 1]),
                     ci32, rscr)
        for src in ((xqb,) if halo else (xnb, xqb)):
            for j in range(8):
                for ch in range(nch):
                    p.op("tensor", lambda e, j=j, ch=ch, src=src:
                         e.transpose(c.PT.ap[:, ch * 128:ch * 128 + 128],
                                     src.ap[:, ch, j * 128:j * 128 + 128],
                                     ident.ap[:]),
                         reads=(src, ident), writes=(c.PT,),
                         inc=(ch == nch - 1))
                if src is xnb:
                    p.op("scalar", lambda e, j=j: e.copy(
                        xnf.ap[:, j, :], c.PT.ap[:]),
                        reads=(c.PT,), writes=(xnf,))
                elif halo:
                    p.op("scalar", lambda e, j=j: e.copy(
                        xq_hf.ap[:, j, :], c.PT.ap[:, 0:16]),
                        reads=(c.PT,), writes=(xq_hf,))
                else:
                    p.op("scalar", lambda e, j=j: e.copy(
                        xqo.ap[:, j, :], c.PT.ap[:]),
                        reads=(c.PT,), writes=(xqo,))
        if halo:
            return
        p.dma("gpsimd", lambda e, col0=col0: e.dma_start(
            out=xq_dram.ap.rearrange("c p t -> p c t")[:, :, col0:col0 + GR],
            in_=xqo.ap[:]),
            reads=(xqo,), writes=(xq_dram,))

        for (wt, bias, out_sb, o_d, scale, kind) in (
                (wk_t, bkc, ko, o_k, 1.0, "k"),
                (wq_t, bqc, qo, o_q, 1.0 / 32.0, "q")):
            for j in range(8):
                pb = c.PS[j % 2]
                fns = [
                    (lambda e, j=j, dc=dc, wt=wt, pb=pb: e.matmul(
                        pb.ap[:], wt.ap[:, dc, j * 128:j * 128 + 128],
                        xnf.ap[:, dc, :], start=(dc == 0), stop=(dc == 7)))
                    for dc in range(8)]
                p.mm_group(fns, reads=(wt, xnf), write=pb)
                p.op("scalar", lambda e, j=j, pb=pb, bias=bias,
                     out_sb=out_sb, scale=scale: e.activation(
                         out_sb.ap[:, j, :], pb.ap[:], AF.Identity,
                         bias=bias.ap[:, j:j + 1], scale=scale),
                     reads=(pb, bias), writes=(out_sb,))
            p.dma("gpsimd", lambda e, col0=col0, o_d=o_d, out_sb=out_sb:
                  e.dma_start(out=o_d.ap.rearrange("c p t -> p c t")
                              [:, :, col0:col0 + GR], in_=out_sb.ap[:]),
                  reads=(out_sb,), writes=(o_d,))
            for j in range(8):
                p.op("scalar", lambda e, j=j, out_sb=out_sb: e.activation(
                    sqt.ap[:, j, :], out_sb.ap[:, j, :], AF.Square),
                    reads=(out_sb,), writes=(sqt,))
            fns = [(lambda e, j=j: e.matmul(
                c.PR.ap[0:1, :], ones_b.ap[:], sqt.ap[:, j, :],
                start=(j == 0), stop=(j == 7))) for j in range(8)]
            p.mm_group(fns, reads=(sqt, ones_b), write=c.PR)
            if kind == "q":
                p.op("scalar", lambda e: e.activation(
                    nrm.ap[0:1, :], c.PR.ap[0:1, :], AF.Sqrt),
                    reads=(c.PR,), writes=(nrm,))
                p.dma("gpsimd", lambda e, col0=col0: e.dma_start(
                    out=o_qn.ap[0:1, col0:col0 + GR], in_=nrm.ap[0:1, :]),
                    reads=(nrm,), writes=(o_qn,))
            else:
                p.op("vector", lambda e: e.tensor_copy(
                    nrm.ap[0:1, :], c.PR.ap[0:1, :]),
                    reads=(c.PR,), writes=(nrm,))
                p.op("vector", lambda e: e.tensor_reduce(
                    kxs.ap[0:1, 1:2], nrm.ap[0:1, :], axis=AX.X, op=OP.max),
                    reads=(nrm,), writes=(kxs,))
                if ti == 0:
                    p.op("vector", lambda e: e.tensor_copy(
                        kxs.ap[0:1, 0:1], kxs.ap[0:1, 1:2]),
                        reads=(kxs,), writes=(kxs,))
                else:
                    p.op("vector", lambda e: e.tensor_tensor(
                        kxs.ap[0:1, 0:1], kxs.ap[0:1, 0:1], kxs.ap[0:1, 1:2],
                        op=OP.max), reads=(kxs,), writes=(kxs,))
        for cc in range(4):
            for h in range(2):
                pb = c.PS[2 + h]
                fns = [
                    (lambda e, cc=cc, h=h, dc=dc, pb=pb: e.matmul(
                        pb.ap[:], xnf.ap[:, dc, cc * 128:cc * 128 + 128],
                        wv_t.ap[:, dc, h * 512:h * 512 + 512],
                        start=(dc == 0), stop=(dc == 7)))
                    for dc in range(8)]
                p.mm_group(fns, reads=(wv_t, xnf), write=pb)
                p.op("vector", lambda e, cc=cc, h=h, pb=pb: e.tensor_tensor(
                    vo.ap[:, cc, h * 512:h * 512 + 512], pb.ap[:],
                    bv_bc.ap[:, h * 512:h * 512 + 512], op=OP.add),
                    reads=(pb, bv_bc), writes=(vo,))
        p.dma("gpsimd", lambda e, ti=ti: e.dma_start(
            out=o_v.ap.rearrange("(c p) d -> p c d", p=128)
            [:, 4 * ti:4 * ti + 4, :], in_=vo.ap[:]),
            reads=(vo,), writes=(o_v,))

    do_tile(8)
    for ti in range(8):
        do_tile(ti)
    p.dma("gpsimd", lambda e: e.dma_start(
        out=o_kx.ap, in_=kxs.ap[0:1, 0:1]), reads=(kxs,), writes=(o_kx,))


def emit_A_ffn(c, I, o_g0, xq_dram, xq_hf, cmax_cols, cmax_hc,
               ident, ident_f, to_col, to_tile, col_to_row, ones_b):
    p = c.p
    ones_col = c.sb("ones_colF", [128, 1])
    p.op("gpsimd", lambda e: e.memset(ones_col.ap[:], 1.0),
         writes=(ones_col,))
    w1n = c.sb("w1n", [128, 8, HID], F8)
    w2n = c.sb("w2n", [128, 32, D], F8)
    mw_sc = c.sb("mw_sc", [1, 2])
    mwcol = c.sb("mwcol", [128, 3])

    def ternarize(widx, wsrc, wdst, nchunk, wcols, wld, wsc, tern_s):
        for ch in range(nchunk):
            p.dma("sync", lambda e, ch=ch: e.dma_start(
                out=wld.ap[:, 0:wcols], in_=wsrc.ap.rearrange(
                    "(c p) o -> p c o", p=128)[:, ch, :]),
                reads=(wsrc,), writes=(wld,))
            p.op("vector", lambda e, ch=ch: e.tensor_reduce(
                wsc.ap[:, ch:ch + 1], wld.ap[:, 0:wcols], axis=AX.X,
                op=OP.add, apply_absolute_value=True),
                reads=(wld,), writes=(wsc,))
        p.op("vector", lambda e, nchunk=nchunk: e.tensor_reduce(
            wsc.ap[:, 32:33], wsc.ap[:, 0:nchunk], axis=AX.X, op=OP.add),
            reads=(wsc,), writes=(wsc,))
        p.op("tensor", lambda e: e.matmul(
            c.PR.ap[0:1, 0:1], ones_col.ap[:], wsc.ap[:, 32:33],
            start=True, stop=True), reads=(wsc, ones_col), writes=(c.PR,))
        p.op("vector", lambda e, widx=widx: e.tensor_scalar(
            mw_sc.ap[0:1, widx:widx + 1], c.PR.ap[0:1, 0:1],
            1.0 / (D * HID), None, op0=OP.mult),
            reads=(c.PR,), writes=(mw_sc,))
        to_col(mwcol, np.s_[:, widx:widx + 1], mw_sc,
               np.s_[0:1, widx:widx + 1])
        p.op("vector", lambda e, widx=widx: e.tensor_scalar(
            mwcol.ap[:, widx:widx + 1], mwcol.ap[:, widx:widx + 1], 0.5, None,
            op0=OP.mult), reads=(mwcol,), writes=(mwcol,))
        for ch in range(nchunk):
            p.dma("sync", lambda e, ch=ch: e.dma_start(
                out=wld.ap[:, 0:wcols], in_=wsrc.ap.rearrange(
                    "(c p) o -> p c o", p=128)[:, ch, :]),
                reads=(wsrc,), writes=(wld,))
            p.op("vector", lambda e, ch=ch, widx=widx: e.tensor_scalar(
                tern_s.ap[:, 0:wcols], wld.ap[:, 0:wcols],
                mwcol.ap[:, widx:widx + 1], None, op0=OP.is_gt),
                reads=(wld, mwcol), writes=(tern_s,))
            p.op("vector", lambda e, ch=ch, widx=widx: e.tensor_scalar(
                wld.ap[:, 0:wcols], wld.ap[:, 0:wcols], -1.0,
                mwcol.ap[:, widx:widx + 1], op0=OP.mult, op1=OP.is_gt),
                reads=(wld, mwcol), writes=(wld,))
            p.op("vector", lambda e, ch=ch, wdst=wdst, wcols=wcols:
                 e.tensor_tensor(
                     wdst.ap[:, ch, :], tern_s.ap[:, 0:wcols],
                     wld.ap[:, 0:wcols], op=OP.subtract),
                 reads=(tern_s, wld), writes=(wdst,))

    with c.scope():
        wld = c.sb("wld", [128, HID])
        wsc = c.sb("wsc", [128, 33])
        tern_s = c.sb("tern_s", [128, HID], BF16)
        ternarize(0, I["w1t"], w1n, 8, HID, wld, wsc, tern_s)
        ternarize(1, I["w2t"], w2n, 32, D, wld, wsc, tern_s)
    to_col(mwcol, np.s_[:, 2:3], mw_sc, np.s_[0:1, 1:2])
    mw1col = c.sb("mw1col", [128, 1])
    to_col(mw1col, np.s_[:, 0:1], mw_sc, np.s_[0:1, 0:1])

    cwc = c.sb("cwc", [128, 32, 3])
    cbc = c.sb("cbc", [128, 32])
    p.dma("sync", lambda e: e.dma_start(out=cwc.ap[:], in_=I["cw_c"].ap),
          reads=(I["cw_c"],), writes=(cwc,))
    p.dma("sync", lambda e: e.dma_start(out=cbc.ap[:], in_=I["cb_c"].ap),
          reads=(I["cb_c"],), writes=(cbc,))

    u_cols = c.sb("u_cols", [128, 32])
    p.op("vector", lambda e: e.tensor_scalar(
        u_cols.ap[:], cmax_cols.ap[:], mw1col.ap[:, 0:1], 1.0 / 127.0,
        op0=OP.mult, op1=OP.mult), reads=(cmax_cols, mw1col),
        writes=(u_cols,))
    uh_col = c.sb("uh_col", [128, 1])
    p.op("vector", lambda e: e.tensor_scalar(
        uh_col.ap[:], cmax_hc.ap[:], mw1col.ap[:, 0:1], 1.0 / 127.0,
        op0=OP.mult, op1=OP.mult), reads=(cmax_hc, mw1col), writes=(uh_col,))
    uh_row = c.sb("uh_row", [1, 16])
    col_to_row(uh_row, np.s_[0:1, 0:16], uh_col, np.s_[:, 0:1], 16)
    uh_bc = c.sb("uh_bc", [128, 16])
    to_tile(uh_bc, np.s_[:, :], uh_row, np.s_[0:1, :], 16)

    h_halo = c.sb("h_halo", [128, 32, 16], FP16)
    for ch in range(32):
        pb = c.PS[ch % 2]
        fns = [(lambda e, ch=ch, dc=dc, pb=pb: e.matmul(
            pb.ap[:, 0:16], w1n.ap[:, dc, ch * 128:ch * 128 + 128],
            xq_hf.ap[:, dc, :], start=(dc == 0), stop=(dc == 7)))
            for dc in range(8)]
        p.mm_group(fns, reads=(w1n, xq_hf), write=pb)
        p.op("vector", lambda e, ch=ch, pb=pb: e.tensor_tensor(
            h_halo.ap[:, ch, :], pb.ap[:, 0:16], uh_bc.ap[:], op=OP.mult),
            reads=(pb, uh_bc), writes=(h_halo,))

    xqf = [c.sb("xqf0", [128, 8, GR], BF16)]
    h_t = c.sb("h_t", [128, 32, GR + 2])
    zq_t = c.sb("zq_t", [128, 32, GR], BF16)
    y_t = c.sb("y_t", [128, GR])
    u_row = c.sb("u_rowA", [1, GR])
    u_bc = c.sb("u_bc", [128, GR])
    m1 = c.sb("m1", [128, GR])
    mneg = c.sb("mneg", [128, GR])
    s2c = c.sb("s2c", [128, 4])
    r2c = c.sb("r2c", [128, 4])
    u2c = c.sb("u2c", [128, 4])
    s2r = c.sb("s2r", [1, GR])
    s2bc = c.sb("s2bc", [128, GR])
    ci2 = c.sb("ci2", [128, GR], I32)
    rs2 = c.sb("rs2", [128, GR])
    g0_o = c.sb("g0_o", [128, 4, D])

    for g in range(NGR):
        col0 = g * GR
        xq_g = xqf[0]
        p.dma("sync", lambda e, col0=col0, xq_g=xq_g: e.dma_start(
            out=xq_g.ap[:],
            in_=xq_dram.ap.rearrange("c p t -> p c t")[:, :, col0:col0 + GR]),
            reads=(xq_dram,), writes=(xq_g,))
        for cc in range(4):
            col_to_row(u_row, np.s_[0:1, cc * 128:cc * 128 + 128], u_cols,
                       np.s_[:, 4 * g + cc:4 * g + cc + 1])
        to_tile(u_bc, np.s_[:, :], u_row, np.s_[0:1, :], GR)
        for ch in range(32):
            pb = c.PS[2 + ch % 2]
            fns = [(lambda e, ch=ch, dc=dc, pb=pb, xq_g=xq_g: e.matmul(
                pb.ap[:], w1n.ap[:, dc, ch * 128:ch * 128 + 128],
                xq_g.ap[:, dc, :], start=(dc == 0), stop=(dc == 7)))
                for dc in range(8)]
            p.mm_group(fns, reads=(w1n, xq_g), write=pb)
            p.op("vector", lambda e, ch=ch, pb=pb: e.tensor_tensor(
                h_t.ap[:, ch, 1:GR + 1], pb.ap[:], u_bc.ap[:], op=OP.mult),
                reads=(pb, u_bc), writes=(h_t,))
        p.op("vector", lambda e, g=g: e.tensor_copy(
            h_t.ap[:, :, 0:1].rearrange("p c o -> p (c o)"),
            h_halo.ap[:, :, 2 * g:2 * g + 1].rearrange("p c o -> p (c o)")),
            reads=(h_halo, h_t), writes=(h_t,))
        p.op("vector", lambda e, g=g: e.tensor_copy(
            h_t.ap[:, :, GR + 1:GR + 2].rearrange("p c o -> p (c o)"),
            h_halo.ap[:, :, 2 * g + 1:2 * g + 2].rearrange("p c o -> p (c o)")),
            reads=(h_halo, h_t), writes=(h_t,))
        for ch in range(32):
            p.op("scalar", lambda e, ch=ch: e.activation(
                y_t.ap[:], h_t.ap[:, ch, 1:GR + 1], AF.Identity,
                bias=cbc.ap[:, ch:ch + 1], scale=cwc.ap[:, ch, 1:2]),
                reads=(h_t, cbc, cwc), writes=(y_t,))
            p.op("vector", lambda e, ch=ch: e.scalar_tensor_tensor(
                y_t.ap[:], h_t.ap[:, ch, 0:GR], cwc.ap[:, ch, 0:1], y_t.ap[:],
                op0=OP.mult, op1=OP.add), reads=(h_t, cwc, y_t),
                writes=(y_t,))
            p.op("vector", lambda e, ch=ch: e.scalar_tensor_tensor(
                y_t.ap[:], h_t.ap[:, ch, 2:GR + 2], cwc.ap[:, ch, 2:3],
                y_t.ap[:], op0=OP.mult, op1=OP.add), reads=(h_t, cwc, y_t),
                writes=(y_t,))
            p.op("scalar", lambda e, ch=ch: e.activation(
                h_t.ap[:, ch, 1:GR + 1], y_t.ap[:], AF.Silu), reads=(y_t, h_t),
                writes=(h_t,))
            if ch == 0:
                p.op("vector", lambda e: e.tensor_scalar(
                    m1.ap[:], h_t.ap[:, 0, 1:GR + 1], -1.0, None,
                    op0=OP.mult), reads=(h_t,), writes=(m1,))
                p.op("vector", lambda e: e.tensor_tensor(
                    m1.ap[:], m1.ap[:], h_t.ap[:, 0, 1:GR + 1], op=OP.max),
                    reads=(m1, h_t), writes=(m1,))
            else:
                p.op("vector", lambda e, ch=ch: e.tensor_scalar(
                    mneg.ap[:], h_t.ap[:, ch, 1:GR + 1], -1.0, None,
                    op0=OP.mult), reads=(h_t,), writes=(mneg,))
                p.op("vector", lambda e, ch=ch: e.tensor_tensor(
                    mneg.ap[:], mneg.ap[:], h_t.ap[:, ch, 1:GR + 1],
                    op=OP.max), reads=(mneg, h_t), writes=(mneg,))
                p.op("vector", lambda e: e.tensor_tensor(
                    m1.ap[:], m1.ap[:], mneg.ap[:], op=OP.max),
                    reads=(m1, mneg), writes=(m1,))
        for cc in range(4):
            p.op("tensor", lambda e, cc=cc: e.transpose(
                c.PR.ap[:, 0:128], m1.ap[:, cc * 128:cc * 128 + 128],
                ident_f.ap[:]), reads=(m1, ident_f), writes=(c.PR,))
            p.op("vector", lambda e, cc=cc: e.tensor_reduce(
                s2c.ap[:, cc:cc + 1], c.PR.ap[:, 0:128], axis=AX.X,
                op=OP.max), reads=(c.PR,), writes=(s2c,))
            p.op("vector", lambda e, cc=cc: e.tensor_scalar(
                s2c.ap[:, cc:cc + 1], s2c.ap[:, cc:cc + 1], 1e-5, None,
                op0=OP.max), reads=(s2c,), writes=(s2c,))
            p.op("vector", lambda e, cc=cc: e.tensor_scalar(
                r2c.ap[:, cc:cc + 1], s2c.ap[:, cc:cc + 1], 1.0 / 127.0, None,
                op0=OP.mult), reads=(s2c,), writes=(r2c,))
            p.op("vector", lambda e, cc=cc: e.reciprocal(
                s2c.ap[:, cc:cc + 1], s2c.ap[:, cc:cc + 1]),
                reads=(s2c,), writes=(s2c,))
            p.op("vector", lambda e, cc=cc: e.tensor_scalar(
                s2c.ap[:, cc:cc + 1], s2c.ap[:, cc:cc + 1], 127.0, None,
                op0=OP.mult), reads=(s2c,), writes=(s2c,))
            col_to_row(s2r, np.s_[0:1, cc * 128:cc * 128 + 128], s2c,
                       np.s_[:, cc:cc + 1])
        to_tile(s2bc, np.s_[:, :], s2r, np.s_[0:1, :], GR)
        p.op("vector", lambda e: e.tensor_scalar(
            u2c.ap[:], r2c.ap[:], mwcol.ap[:, 2:3], None, op0=OP.mult),
            reads=(r2c, mwcol), writes=(u2c,))
        for ch in range(32):
            round_to(c, zq_t, np.s_[:, ch, :], h_t, np.s_[:, ch, 1:GR + 1],
                     (s2bc, np.s_[:, :]), None, ci2, rs2, s_is_tile=True)
        for cc in range(4):
            for h in range(2):
                pb = c.PS[4 + h]
                fns = [(lambda e, cc=cc, h=h, ch=ch, pb=pb: e.matmul(
                    pb.ap[:], zq_t.ap[:, ch, cc * 128:cc * 128 + 128],
                    w2n.ap[:, ch, h * 512:h * 512 + 512],
                    start=(ch == 0), stop=(ch == 31)))
                    for ch in range(32)]
                p.mm_group(fns, reads=(zq_t, w2n), write=pb)
                p.op("scalar", lambda e, cc=cc, h=h, pb=pb: e.activation(
                    g0_o.ap[:, cc, h * 512:h * 512 + 512], pb.ap[:], AF.Silu,
                    scale=u2c.ap[:, cc:cc + 1]), reads=(pb, u2c),
                    writes=(g0_o,))
        p.dma("gpsimd", lambda e, g=g: e.dma_start(
            out=o_g0.ap.rearrange("(c p) d -> p c d", p=128)
            [:, 4 * g:4 * g + 4, :], in_=g0_o.ap[:]),
            reads=(g0_o,), writes=(o_g0,))



def _col8(v):
    return np.ascontiguousarray(v.reshape(8, 128).T)


def shard_rows(x_b, par):
    return np.ascontiguousarray(
        x_b.reshape(16, GR, *x_b.shape[1:])[par::2]
        .reshape(TOK, *x_b.shape[1:]))


def unshard_rows(own0, own1):
    out = np.empty((16, GR) + own0.shape[1:], own0.dtype)
    out[0::2] = own0.reshape(8, GR, *own0.shape[1:])
    out[1::2] = own1.reshape(8, GR, *own1.shape[1:])
    return out.reshape(T, *own0.shape[1:])


def halo_rows(x_b, par):
    out = np.zeros((16, x_b.shape[1]), x_b.dtype)
    for gl in range(8):
        G = 2 * gl + par
        lo, hi = G * GR - 1, G * GR + GR
        if lo >= 0:
            out[2 * gl] = x_b[lo]
        if hi < T:
            out[2 * gl + 1] = x_b[hi]
    return out


def launch_A_inputs(inputs):
    x = np.asarray(inputs["L0_attn_x"], np.float32)
    com = dict(
        wq=np.asarray(inputs["L0_attn_Wq"], np.float32),
        wk=np.asarray(inputs["L0_attn_Wk"], np.float32),
        wv=np.asarray(inputs["L0_attn_Wv"], np.float32),
        bq_c=_col8(np.asarray(inputs["L0_attn_bq"], np.float32)),
        bk_c=_col8(np.asarray(inputs["L0_attn_bk"], np.float32)),
        bv_r=np.asarray(inputs["L0_attn_bv"], np.float32).reshape(1, D),
        w1t=np.ascontiguousarray(np.asarray(inputs["L0_ffn_w1"],
                                            np.float32).T),
        w2t=np.ascontiguousarray(np.asarray(inputs["L0_ffn_w2"],
                                            np.float32).T),
        cw_c=np.ascontiguousarray(
            np.asarray(inputs["L0_ffn_conv_w"], np.float32)
            .reshape(32, 128, 3).transpose(1, 0, 2)),
        cb_c=np.ascontiguousarray(
            np.asarray(inputs["L0_ffn_conv_b"], np.float32)
            .reshape(32, 128).T),
    )
    maps = []
    for core in range(8):
        b, par = core // 2, core % 2
        m = dict(com)
        m["xg"] = shard_rows(x[b], par)
        m["xh"] = halo_rows(x[b], par)
        maps.append(m)
    return maps


def kernel(**inputs):
    raise NotImplementedError("launches B-D pending")



def build_launch_B():
    from contextlib import ExitStack
    nc = bass.Bass("TRN2", target_bir_lowering=False)
    qf_d = Buf(nc.dram_tensor("qf", [8, 128, TOK], BF16,
                              kind="ExternalInput").ap(), "qf")
    kf_d = Buf(nc.dram_tensor("kf", [8, 128, T], BF16,
                              kind="ExternalInput").ap(), "kf")
    vt_d = Buf(nc.dram_tensor("vt", [T, D], BF16,
                              kind="ExternalInput").ap(), "vt")
    qn_d = Buf(nc.dram_tensor("qn_c", [128, 32], F32,
                              kind="ExternalInput").ap(), "qn_c")
    kx_d = Buf(nc.dram_tensor("kx", [1, 2], F32,
                              kind="ExternalInput").ap(), "kx")
    par_d = Buf(nc.dram_tensor("par", [1, 1], F32,
                               kind="ExternalInput").ap(), "par")
    o_acc = Buf(nc.dram_tensor("o_acc", [TOK, D], F32,
                               kind="ExternalOutput").ap(), "o_acc")
    o_l = Buf(nc.dram_tensor("o_l", [128, 32], F32,
                             kind="ExternalOutput").ap(), "o_l")
    o_mh = Buf(nc.dram_tensor("o_mh", [128, 32], F32,
                              kind="ExternalOutput").ap(), "o_mh")

    with ExitStack() as st:
        c = Ctx(nc, st)
        p = c.p
        ident = make_identity(c)
        ident_f = make_identity(c, "ident_f", F32)
        to_col, to_tile, col_to_row = bcaster(c, ident_f)

        qf = c.sb("qf_sb", [128, 8, TOK], BF16)
        p.dma("sync", lambda e: e.dma_start(
            out=qf.ap[:], in_=qf_d.ap.rearrange("c p t -> p c t")),
            reads=(qf_d,), writes=(qf,))
        kxs = c.sb("kxs", [1, 2])
        p.dma("sync", lambda e: e.dma_start(out=kxs.ap[:], in_=kx_d.ap),
              reads=(kx_d,), writes=(kxs,))
        p.op("vector", lambda e: e.tensor_reduce(
            kxs.ap[0:1, 0:1], kxs.ap[0:1, 0:2], axis=AX.X, op=OP.max),
            reads=(kxs,), writes=(kxs,))
        p.op("scalar", lambda e: e.activation(
            kxs.ap[0:1, 0:1], kxs.ap[0:1, 0:1], AF.Sqrt),
            reads=(kxs,), writes=(kxs,))
        mkcol = c.sb("mkcol", [128, 1])
        to_col(mkcol, np.s_[:, 0:1], kxs, np.s_[0:1, 0:1])
        qnc = c.sb("qnc", [128, 32])
        p.dma("sync", lambda e: e.dma_start(out=qnc.ap[:], in_=qn_d.ap),
              reads=(qn_d,), writes=(qnc,))
        mh = c.sb("mh", [128, 32])
        nmh = c.sb("nmh", [128, 32])
        p.op("vector", lambda e: e.tensor_scalar(
            mh.ap[:], qnc.ap[:], mkcol.ap[:, 0:1], None, op0=OP.mult),
            reads=(qnc, mkcol), writes=(mh,))
        p.op("vector", lambda e: e.tensor_scalar(
            nmh.ap[:], mh.ap[:], -1.0, None, op0=OP.mult),
            reads=(mh,), writes=(nmh,))
        p.dma("gpsimd", lambda e: e.dma_start(out=o_mh.ap, in_=mh.ap[:]),
              reads=(mh,), writes=(o_mh,))
        parc = c.sb("parc", [1, 1])
        p.dma("sync", lambda e: e.dma_start(out=parc.ap[:], in_=par_d.ap),
              reads=(par_d,), writes=(parc,))
        par_col = c.sb("par_col", [128, 1])
        to_col(par_col, np.s_[:, 0:1], parc, np.s_[0:1, 0:1])
        pat = c.sb("pat", [128, 512])
        patneg = c.sb("patneg", [128, 512])
        m1m = [c.sb(f"m1m{j}", [128, 512], BF16) for j in range(4)]
        m2m = [c.sb(f"m2m{j}", [128, 512], BF16) for j in range(4)]
        for j in range(4):
            p.op("gpsimd", lambda e: e.memset(pat.ap[:], 1.0), writes=(pat,))
            p.op("gpsimd", lambda e, j=j: e.affine_select(
                out=pat.ap[:], in_=pat.ap[:], compare_op=OP.is_ge, fill=0.0,
                base=128 * j, pattern=[[-1, 512]], channel_multiplier=1),
                reads=(pat,), writes=(pat,))
            p.op("vector", lambda e, j=j: e.tensor_scalar(
                patneg.ap[:], pat.ap[:], -1.0, 1.0, op0=OP.mult, op1=OP.add),
                reads=(pat,), writes=(patneg,))
            p.op("vector", lambda e, j=j: e.scalar_tensor_tensor(
                m1m[j].ap[:], patneg.ap[:], par_col.ap[:, 0:1], pat.ap[:],
                op0=OP.mult, op1=OP.add), reads=(patneg, par_col, pat),
                writes=(m1m[j],))
            p.op("vector", lambda e, j=j: e.tensor_scalar(
                m2m[j].ap[:], pat.ap[:], par_col.ap[:, 0:1], None,
                op0=OP.mult), reads=(pat, par_col), writes=(m2m[j],))

        kb = [c.sb(f"kb{i}", [128, 8, 1024], BF16) for i in range(2)]
        vb = [c.sb(f"vb{i}", [128, 8, 1024], BF16) for i in range(2)]
        acc = c.sb("acc", [128, 4, D])
        lcol = c.sb("lcol", [128, 4])
        lall = c.sb("lall", [128, 32])
        p.op("gpsimd", lambda e: e.memset(lall.ap[:], 0.0), writes=(lall,))
        lb = c.sb("lb", [128, 1])
        e_sb = c.sb("e_sb", [128, 512], BF16)
        eT = c.sb("eT", [128, 512], BF16)

        for g in range(NGR):
            nsup = g + 1
            p.op("gpsimd", lambda e: e.memset(acc.ap[:], 0.0), writes=(acc,))
            p.op("gpsimd", lambda e: e.memset(lcol.ap[:], 0.0), writes=(lcol,))
            for s in range(nsup):
                kbuf, vbuf = kb[s % 2], vb[s % 2]
                off = s * 1024
                p.dma("sync", lambda e, off=off, kbuf=kbuf: e.dma_start(
                    out=kbuf.ap[:], in_=kf_d.ap.rearrange("c p t -> p c t")
                    [:, :, off:off + 1024]), reads=(kf_d,), writes=(kbuf,))
                p.dma("scalar", lambda e, s=s, vbuf=vbuf: e.dma_start(
                    out=vbuf.ap[:], in_=vt_d.ap.rearrange(
                        "(c p) d -> p c d", p=128)
                    [:, 8 * s:8 * s + 8, :]), reads=(vt_d,), writes=(vbuf,))
                for j in range(NJ):
                    qs = np.s_[g * GR + j * 128: g * GR + j * 128 + 128]
                    first_av = [True]
                    for blk in range(2):
                        gb = 2 * s + blk
                        pbS = c.PS[blk % 2]
                        fns = [(lambda e, dc=dc, blk=blk, pbS=pbS, qs=qs,
                                kbuf=kbuf: e.matmul(
                                    pbS.ap[:], qf.ap[:, dc, qs],
                                    kbuf.ap[:, dc,
                                            blk * 512:blk * 512 + 512],
                                    start=(dc == 0), stop=(dc == 7)))
                               for dc in range(8)]
                        p.mm_group(fns, reads=(qf, kbuf), write=pbS)
                        masked = gb in (2 * g, 2 * g + 1)
                        if masked:
                            p.op("scalar", lambda e, pbS=pbS, g=g, j=j:
                                 e.activation(
                                     e_sb.ap[:], pbS.ap[:], AF.Exp,
                                     bias=nmh.ap[:, 4 * g + j:4 * g + j + 1]),
                                 reads=(pbS, nmh), writes=(e_sb,))
                            mm = m1m[j] if gb == 2 * g else m2m[j]
                            p.op("vector", lambda e, mm=mm: e.tensor_tensor(
                                e_sb.ap[:], e_sb.ap[:], mm.ap[:], op=OP.mult),
                                reads=(e_sb, mm), writes=(e_sb,))
                            p.op("vector", lambda e: e.tensor_reduce(
                                lb.ap[:], e_sb.ap[:], axis=AX.X, op=OP.add),
                                reads=(e_sb,), writes=(lb,))
                        else:
                            p.op("scalar", lambda e, pbS=pbS, g=g, j=j:
                                 e.activation(
                                     e_sb.ap[:], pbS.ap[:], AF.Exp,
                                     bias=nmh.ap[:, 4 * g + j:4 * g + j + 1],
                                     accum_out=lb.ap[:]),
                                 reads=(pbS, nmh), writes=(e_sb, lb))
                        p.op("vector", lambda e, j=j: e.tensor_tensor(
                            lcol.ap[:, j:j + 1], lcol.ap[:, j:j + 1],
                            lb.ap[:], op=OP.add), reads=(lcol, lb),
                            writes=(lcol,))
                        for cc in range(4):
                            p.op("tensor", lambda e, cc=cc: e.transpose(
                                c.PT.ap[:, cc * 128:cc * 128 + 128],
                                e_sb.ap[:, cc * 128:cc * 128 + 128],
                                ident.ap[:]), reads=(e_sb, ident),
                                writes=(c.PT,), inc=(cc == 3))
                        p.op("scalar", lambda e: e.copy(eT.ap[:], c.PT.ap[:]),
                             reads=(c.PT,), writes=(eT,))
                        for h in range(2):
                            pbA = c.PS[4 + h]
                            for cc in range(4):
                                strt = first_av[0] and cc == 0
                                stp = (blk == 1 and cc == 3)
                                deps = ()
                                if strt:
                                    deps = tuple(pbA.readers) + \
                                        ((pbA.writer,) if pbA.writer else ())
                                p.op("tensor",
                                     lambda e, h=h, cc=cc, blk=blk, pbA=pbA,
                                     vbuf=vbuf, strt=strt, stp=stp: e.matmul(
                                         pbA.ap[:], eT.ap[:, cc * 128:
                                                          cc * 128 + 128],
                                         vbuf.ap[:, blk * 4 + cc,
                                                 h * 512:h * 512 + 512],
                                         start=strt, stop=stp,
                                         skip_group_check=True),
                                     reads=(eT, vbuf), deps=deps)
                        first_av[0] = False
                    for h in range(2):
                        pbA = c.PS[4 + h]
                        tok = (("tensor", p.cnt["tensor"]))
                        pbA.writer = tok
                        pbA.readers = []
                        if s == 0:
                            p.op("vector", lambda e, h=h, j=j, pbA=pbA:
                                 e.tensor_copy(
                                     acc.ap[:, j, h * 512:h * 512 + 512],
                                     pbA.ap[:]), reads=(pbA,), writes=(acc,))
                        else:
                            p.op("vector", lambda e, h=h, j=j, pbA=pbA:
                                 e.tensor_tensor(
                                     acc.ap[:, j, h * 512:h * 512 + 512],
                                     pbA.ap[:],
                                     acc.ap[:, j, h * 512:h * 512 + 512],
                                     op=OP.add), reads=(pbA, acc),
                                 writes=(acc,))
            p.op("vector", lambda e, g=g: e.tensor_copy(
                lall.ap[:, 4 * g:4 * g + 4], lcol.ap[:]),
                reads=(lcol,), writes=(lall,))
            p.dma("gpsimd", lambda e, g=g: e.dma_start(
                out=o_acc.ap.rearrange("(c p) d -> p c d", p=128)
                [:, 4 * g:4 * g + 4, :], in_=acc.ap[:]),
                reads=(acc,), writes=(o_acc,))
        p.dma("gpsimd", lambda e: e.dma_start(out=o_l.ap, in_=lall.ap[:]),
              reads=(lall,), writes=(o_l,))
        p.emit()
    return nc


def launch_B_inputs(inputs, resA):
    maps = []
    for core in range(8):
        b, par = core // 2, core % 2
        r0, r1 = resA[2 * b], resA[2 * b + 1]
        kf = np.empty((8, 128, T), ml_dtypes.bfloat16)
        kv = kf.reshape(8, 128, 16, GR)
        kv[:, :, 0::2] = np.asarray(r0["o_k"]).reshape(8, 128, 8, GR)
        kv[:, :, 1::2] = np.asarray(r1["o_k"]).reshape(8, 128, 8, GR)
        vt = unshard_rows(np.asarray(r0["o_v"]), np.asarray(r1["o_v"]))
        rc = resA[core]
        maps.append(dict(
            qf=np.asarray(rc["o_q"]),
            kf=kf,
            vt=vt,
            qn_c=np.ascontiguousarray(
                np.asarray(rc["o_qn"], np.float32).reshape(32, 128).T),
            kx=np.array([[float(np.asarray(r0["o_kx"]).ravel()[0]),
                          float(np.asarray(r1["o_kx"]).ravel()[0])]],
                        np.float32),
            par=np.array([[float(par)]], np.float32),
        ))
    return maps



def build_launch_C():
    from contextlib import ExitStack
    nc = bass.Bass("TRN2", target_bir_lowering=False)

    def in_d(nm, sh, dt=F32):
        return Buf(nc.dram_tensor(nm, sh, dt, kind="ExternalInput").ap(), nm)

    def out_d(nm, sh, dt=F32):
        return Buf(nc.dram_tensor(nm, sh, dt, kind="ExternalOutput").ap(), nm)

    acc_d = in_d("acc", [TOK, D])
    l_d = in_d("l_all", [128, 64])
    mh_d = in_d("mh_all", [128, 64])
    g0_d = in_d("g0", [TOK, D])
    xg_d = in_d("xg", [TOK, D])
    qw_d = in_d("qkv_wT", [D, 3 * D])
    qb_d = in_d("qkv_b", [1, 3 * D])
    gam_d = in_d("gam_c", [128, 8])
    bet_d = in_d("bet_c", [128, 8])

    o_x1 = out_d("o_x1", [TOK, D])
    o_xn1 = out_d("o_xn1", [TOK, D])
    o_v1 = out_d("o_v1", [8, 128, TOK], BF16)
    o_s1p = out_d("o_s1p", [8, 128, D])
    o_am = out_d("o_am", [1, 1])

    with ExitStack() as st:
        c = Ctx(nc, st)
        p = c.p
        ident = make_identity(c)
        ident_f = make_identity(c, "ident_f", F32)
        to_col, to_tile, col_to_row = bcaster(c, ident_f)
        eps_c = c.sb("eps_c", [128, 1])
        p.op("gpsimd", lambda e: e.memset(eps_c.ap[:], EPS), writes=(eps_c,))

        lall = c.sb("lall", [128, 64])
        mhall = c.sb("mhall", [128, 64])
        p.dma("sync", lambda e: e.dma_start(out=lall.ap[:], in_=l_d.ap),
              reads=(l_d,), writes=(lall,))
        p.dma("sync", lambda e: e.dma_start(out=mhall.ap[:], in_=mh_d.ap),
              reads=(mh_d,), writes=(mhall,))
        sc1 = c.sb("sc1", [128, 1])
        row1 = c.sb("row1", [1, 128])
        MZ = c.sb("MZ", [1, 4])
        p.op("vector", lambda e: e.tensor_reduce(
            sc1.ap[:], mhall.ap[:], axis=AX.X, op=OP.max),
            reads=(mhall,), writes=(sc1,))
        col_to_row(row1, np.s_[0:1, :], sc1, np.s_[:, 0:1])
        p.op("vector", lambda e: e.tensor_reduce(
            MZ.ap[0:1, 0:1], row1.ap[0:1, :], axis=AX.X, op=OP.max),
            reads=(row1,), writes=(MZ,))
        negM = c.sb("negM", [128, 1])
        to_col(negM, np.s_[:, 0:1], MZ, np.s_[0:1, 0:1])
        p.op("vector", lambda e: e.tensor_scalar(
            negM.ap[:], negM.ap[:], -1.0, None, op0=OP.mult),
            reads=(negM,), writes=(negM,))
        zt = c.sb("zt", [128, 64])
        p.op("scalar", lambda e: e.activation(
            zt.ap[:], mhall.ap[:], AF.Exp, bias=negM.ap[:, 0:1]),
            reads=(mhall, negM), writes=(zt,))
        p.op("vector", lambda e: e.tensor_tensor(
            zt.ap[:], zt.ap[:], lall.ap[:], op=OP.mult),
            reads=(zt, lall), writes=(zt,))
        p.op("vector", lambda e: e.tensor_reduce(
            sc1.ap[:], zt.ap[:], axis=AX.X, op=OP.add),
            reads=(zt,), writes=(sc1,))
        col_to_row(row1, np.s_[0:1, :], sc1, np.s_[:, 0:1])
        p.op("vector", lambda e: e.tensor_reduce(
            MZ.ap[0:1, 1:2], row1.ap[0:1, :], axis=AX.X, op=OP.add),
            reads=(row1,), writes=(MZ,))
        p.op("vector", lambda e: e.reciprocal(
            MZ.ap[0:1, 2:3], MZ.ap[0:1, 1:2]), reads=(MZ,), writes=(MZ,))
        rZcol = c.sb("rZcol", [128, 1])
        to_col(rZcol, np.s_[:, 0:1], MZ, np.s_[0:1, 2:3])
        escale = c.sb("escale", [128, 32])
        p.op("scalar", lambda e: e.activation(
            escale.ap[:], mhall.ap[:, 0:32], AF.Exp, bias=negM.ap[:, 0:1]),
            reads=(mhall, negM), writes=(escale,))
        p.op("vector", lambda e: e.tensor_scalar(
            escale.ap[:], escale.ap[:], rZcol.ap[:, 0:1], None, op0=OP.mult),
            reads=(escale, rZcol), writes=(escale,))

        wts = c.sb("wts", [128, 8, 3 * D], BF16)
        p.dma("gpsimd", lambda e: e.dma_start(
            out=wts.ap[:], in_=qw_d.ap.rearrange("(c p) o -> p c o", p=128)),
            reads=(qw_d,), writes=(wts,))
        gamc = c.sb("gamc", [128, 8])
        betc = c.sb("betc", [128, 8], BF16)
        p.dma("sync", lambda e: e.dma_start(out=gamc.ap[:], in_=gam_d.ap),
              reads=(gam_d,), writes=(gamc,))
        p.dma("gpsimd", lambda e: e.dma_start(out=betc.ap[:], in_=bet_d.ap),
              reads=(bet_d,), writes=(betc,))
        bprow = c.sb("bprow", [1, 3 * D])
        p.dma("sync", lambda e: e.dma_start(out=bprow.ap[:], in_=qb_d.ap),
              reads=(qb_d,), writes=(bprow,))
        for ns in range(6):
            fns = [(lambda e, ns=ns, dc=dc: e.matmul(
                c.PR.ap[0:1, :], betc.ap[:, dc:dc + 1],
                wts.ap[:, dc, ns * 512:ns * 512 + 512],
                start=(dc == 0), stop=(dc == 7))) for dc in range(8)]
            p.mm_group(fns, reads=(betc, wts), write=c.PR)
            p.op("vector", lambda e, ns=ns: e.tensor_tensor(
                bprow.ap[0:1, ns * 512:ns * 512 + 512],
                bprow.ap[0:1, ns * 512:ns * 512 + 512],
                c.PR.ap[0:1, :], op=OP.add), reads=(c.PR, bprow),
                writes=(bprow,))
        for dc in range(8):
            p.op("scalar", lambda e, dc=dc: e.activation(
                wts.ap[:, dc, :], wts.ap[:, dc, :], AF.Identity,
                scale=gamc.ap[:, dc:dc + 1]), reads=(wts, gamc),
                writes=(wts,))
        bq_bc = c.sb("bq_bc", [128, D])
        bk_bc = c.sb("bk_bc", [128, D])
        ISQ = float(1.0 / np.sqrt(np.float32(T)))
        for hh in range(2):
            to_tile(bq_bc, np.s_[:, hh * 512:hh * 512 + 512], bprow,
                    np.s_[0:1, hh * 512:hh * 512 + 512], 512)
            to_tile(bk_bc, np.s_[:, hh * 512:hh * 512 + 512], bprow,
                    np.s_[0:1, D + hh * 512:D + hh * 512 + 512], 512)
        p.op("vector", lambda e: e.tensor_scalar(
            bk_bc.ap[:], bk_bc.ap[:], ISQ, None, op0=OP.mult),
            reads=(bk_bc,), writes=(bk_bc,))
        bv_c = c.sb("bv_c", [128, 8])
        for dc in range(8):
            p.op("tensor", lambda e, dc=dc: e.transpose(
                c.PR.ap[:, 0:1],
                bprow.ap[0:1, 2 * D + dc * 128:2 * D + dc * 128 + 128],
                ident_f.ap[0:1, 0:1]), reads=(bprow, ident_f),
                writes=(c.PR,))
            p.op("vector", lambda e, dc=dc: e.tensor_copy(
                bv_c.ap[:, dc:dc + 1], c.PR.ap[:, 0:1]),
                reads=(c.PR,), writes=(bv_c,))

        ach = c.sb("ach", [128, D])
        gch = c.sb("gch", [128, D])
        xch = c.sb("xch", [128, D])
        x1ch = c.sb("x1ch", [128, D])
        xn1ch = c.sb("xn1ch", [128, D])
        stat6 = c.sb("stat6", [128, 12])
        mv = c.sb("mv", [128, 2])
        rstd = c.sb("rstd", [128, 1])
        nmb = c.sb("nmb", [128, 1])
        amx = c.sb("amx", [128, 2])
        h1b = c.sb("h1b", [128, 4, D], BF16)
        h1f = c.sb("h1f", [128, 8, GR], BF16)
        q1b = c.sb("q1b", [128, 4, D], BF16)
        k1b = c.sb("k1b", [128, 4, D], BF16)
        v1b = c.sb("v1b", [128, 8, GR], BF16)
        s1s = c.sb("s1s", [128, 8, D])

        for ti in range(8):
            for ch in range(4):
                gi = 4 * ti + ch
                rs = np.s_[:, gi, :]
                p.dma("sync", lambda e, gi=gi: e.dma_start(
                    out=ach.ap[:], in_=acc_d.ap.rearrange(
                        "(c p) d -> p c d", p=128)[:, gi, :]),
                    reads=(acc_d, ach), writes=(ach,))
                p.dma("sync", lambda e, gi=gi: e.dma_start(
                    out=gch.ap[:], in_=g0_d.ap.rearrange(
                        "(c p) d -> p c d", p=128)[:, gi, :]),
                    reads=(g0_d, gch), writes=(gch,))
                p.dma("scalar", lambda e, gi=gi: e.dma_start(
                    out=xch.ap[:], in_=xg_d.ap.rearrange(
                        "(c p) d -> p c d", p=128)[:, gi, :]),
                    reads=(xg_d, xch), writes=(xch,))
                p.op("scalar", lambda e, gi=gi: e.activation(
                    ach.ap[:], ach.ap[:], AF.Identity,
                    scale=escale.ap[:, gi:gi + 1]), reads=(ach, escale),
                    writes=(ach,))
                p.op("vector", lambda e: e.tensor_tensor(
                    ach.ap[:], ach.ap[:], gch.ap[:], op=OP.mult),
                    reads=(ach, gch), writes=(ach,))
                p.op("vector", lambda e: e.tensor_tensor(
                    x1ch.ap[:], ach.ap[:], xch.ap[:], op=OP.add),
                    reads=(ach, xch), writes=(x1ch,))
                p.dma("gpsimd", lambda e, gi=gi: e.dma_start(
                    out=o_x1.ap.rearrange("(c p) d -> p c d", p=128)
                    [:, gi, :], in_=x1ch.ap[:]), reads=(x1ch,),
                    writes=(o_x1,))
                p.op("vector", lambda e: e.bn_stats(
                    stat6.ap[:, 0:6], x1ch.ap[:, 0:512]),
                    reads=(x1ch,), writes=(stat6,))
                p.op("vector", lambda e: e.bn_stats(
                    stat6.ap[:, 6:12], x1ch.ap[:, 512:1024]),
                    reads=(x1ch,), writes=(stat6,))
                p.op("vector", lambda e: e.bn_aggr(mv.ap[:], stat6.ap[:]),
                     reads=(stat6,), writes=(mv,))
                p.op("scalar", lambda e: e.activation(
                    rstd.ap[:], mv.ap[:, 1:2], AF.Sqrt,
                    bias=eps_c.ap[:, 0:1]), reads=(mv, eps_c), writes=(rstd,))
                p.op("vector", lambda e: e.reciprocal(
                    rstd.ap[:], rstd.ap[:]), reads=(rstd,), writes=(rstd,))
                p.op("vector", lambda e: e.tensor_scalar(
                    nmb.ap[:], mv.ap[:, 0:1], rstd.ap[:, 0:1], -1.0,
                    op0=OP.mult, op1=OP.mult), reads=(mv, rstd),
                    writes=(nmb,))
                p.op("scalar", lambda e: e.activation(
                    xn1ch.ap[:], x1ch.ap[:], AF.Identity,
                    bias=nmb.ap[:, 0:1], scale=rstd.ap[:, 0:1]),
                    reads=(x1ch, nmb, rstd), writes=(xn1ch,))
                p.dma("gpsimd", lambda e, gi=gi: e.dma_start(
                    out=o_xn1.ap.rearrange("(c p) d -> p c d", p=128)
                    [:, gi, :], in_=xn1ch.ap[:]), reads=(xn1ch,),
                    writes=(o_xn1,))
                p.op("vector", lambda e: e.tensor_reduce(
                    amx.ap[:, 1:2], xn1ch.ap[:], axis=AX.X, op=OP.max,
                    apply_absolute_value=True), reads=(xn1ch,), writes=(amx,))
                if ti == 0 and ch == 0:
                    p.op("vector", lambda e: e.tensor_copy(
                        amx.ap[:, 0:1], amx.ap[:, 1:2]),
                        reads=(amx,), writes=(amx,))
                else:
                    p.op("vector", lambda e: e.tensor_tensor(
                        amx.ap[:, 0:1], amx.ap[:, 0:1], amx.ap[:, 1:2],
                        op=OP.max), reads=(amx,), writes=(amx,))
                p.op("vector", lambda e: e.bn_stats(
                    stat6.ap[:, 0:6], xn1ch.ap[:, 0:512]),
                    reads=(xn1ch,), writes=(stat6,))
                p.op("vector", lambda e: e.bn_stats(
                    stat6.ap[:, 6:12], xn1ch.ap[:, 512:1024]),
                    reads=(xn1ch,), writes=(stat6,))
                p.op("vector", lambda e: e.bn_aggr(mv.ap[:], stat6.ap[:]),
                     reads=(stat6,), writes=(mv,))
                p.op("scalar", lambda e: e.activation(
                    rstd.ap[:], mv.ap[:, 1:2], AF.Sqrt,
                    bias=eps_c.ap[:, 0:1]), reads=(mv, eps_c), writes=(rstd,))
                p.op("vector", lambda e: e.reciprocal(
                    rstd.ap[:], rstd.ap[:]), reads=(rstd,), writes=(rstd,))
                p.op("vector", lambda e: e.tensor_scalar(
                    nmb.ap[:], mv.ap[:, 0:1], rstd.ap[:, 0:1], -1.0,
                    op0=OP.mult, op1=OP.mult), reads=(mv, rstd),
                    writes=(nmb,))
                p.op("scalar", lambda e, ch=ch: e.activation(
                    h1b.ap[:, ch, :], xn1ch.ap[:], AF.Identity,
                    bias=nmb.ap[:, 0:1], scale=rstd.ap[:, 0:1]),
                    reads=(xn1ch, nmb, rstd), writes=(h1b,))
            for j in range(8):
                for ch in range(4):
                    p.op("tensor", lambda e, j=j, ch=ch: e.transpose(
                        c.PT.ap[:, ch * 128:ch * 128 + 128],
                        h1b.ap[:, ch, j * 128:j * 128 + 128], ident.ap[:]),
                        reads=(h1b, ident), writes=(c.PT,), inc=(ch == 3))
                p.op("scalar", lambda e, j=j: e.copy(
                    h1f.ap[:, j, :], c.PT.ap[:]), reads=(c.PT,),
                    writes=(h1f,))
            for (dst, coff, bbc, wsc) in ((q1b, 0, bq_bc, 1.0),
                                          (k1b, D, bk_bc, ISQ)):
                for cc in range(4):
                    for hh in range(2):
                        pb = c.PS[2 * (cc % 2) + hh]
                        fns = [(lambda e, cc=cc, hh=hh, dc=dc, pb=pb,
                                coff=coff: e.matmul(
                                    pb.ap[:],
                                    h1f.ap[:, dc, cc * 128:cc * 128 + 128],
                                    wts.ap[:, dc, coff + hh * 512:
                                           coff + hh * 512 + 512],
                                    start=(dc == 0), stop=(dc == 7)))
                               for dc in range(8)]
                        p.mm_group(fns, reads=(h1f, wts), write=pb)
                        if wsc == 1.0:
                            p.op("vector", lambda e, cc=cc, hh=hh, pb=pb,
                                 dst=dst, bbc=bbc: e.tensor_tensor(
                                     dst.ap[:, cc, hh * 512:hh * 512 + 512],
                                     pb.ap[:],
                                     bbc.ap[:, hh * 512:hh * 512 + 512],
                                     op=OP.add), reads=(pb, bbc),
                                 writes=(dst,))
                        else:
                            p.op("vector", lambda e, cc=cc, hh=hh, pb=pb,
                                 dst=dst, bbc=bbc, wsc=wsc:
                                 e.scalar_tensor_tensor(
                                     dst.ap[:, cc, hh * 512:hh * 512 + 512],
                                     pb.ap[:], wsc,
                                     bbc.ap[:, hh * 512:hh * 512 + 512],
                                     op0=OP.mult, op1=OP.add),
                                 reads=(pb, bbc), writes=(dst,))
            for vf in range(8):
                pb = c.PS[4 + vf % 2]
                fns = [(lambda e, vf=vf, dc=dc, pb=pb: e.matmul(
                    pb.ap[:], wts.ap[:, dc, 2 * D + vf * 128:
                                     2 * D + vf * 128 + 128],
                    h1f.ap[:, dc, :], start=(dc == 0), stop=(dc == 7)))
                    for dc in range(8)]
                p.mm_group(fns, reads=(wts, h1f), write=pb)
                p.op("scalar", lambda e, vf=vf, pb=pb: e.activation(
                    v1b.ap[:, vf, :], pb.ap[:], AF.Identity,
                    bias=bv_c.ap[:, vf:vf + 1]), reads=(pb, bv_c),
                    writes=(v1b,))
            p.dma("gpsimd", lambda e, ti=ti: e.dma_start(
                out=o_v1.ap.rearrange("c p t -> p c t")
                [:, :, ti * GR:ti * GR + GR], in_=v1b.ap[:]),
                reads=(v1b,), writes=(o_v1,))
            for qc in range(8):
                for hh in range(2):
                    pb = c.PS[2 * (qc % 2) + hh]
                    fns = [(lambda e, qc=qc, hh=hh, cc=cc, pb=pb: e.matmul(
                        pb.ap[:], q1b.ap[:, cc, qc * 128:qc * 128 + 128],
                        k1b.ap[:, cc, hh * 512:hh * 512 + 512],
                        start=(cc == 0), stop=(cc == 3)))
                        for cc in range(4)]
                    p.mm_group(fns, reads=(q1b, k1b), write=pb)
                    if ti == 0:
                        p.op("vector", lambda e, qc=qc, hh=hh, pb=pb:
                             e.tensor_copy(
                                 s1s.ap[:, qc, hh * 512:hh * 512 + 512],
                                 pb.ap[:]), reads=(pb,), writes=(s1s,))
                    else:
                        p.op("vector", lambda e, qc=qc, hh=hh, pb=pb:
                             e.tensor_tensor(
                                 s1s.ap[:, qc, hh * 512:hh * 512 + 512],
                                 pb.ap[:],
                                 s1s.ap[:, qc, hh * 512:hh * 512 + 512],
                                 op=OP.add), reads=(pb, s1s), writes=(s1s,))
        p.dma("gpsimd", lambda e: e.dma_start(
            out=o_s1p.ap.rearrange("c p t -> p c t"), in_=s1s.ap[:]),
            reads=(s1s,), writes=(o_s1p,))
        col_to_row(row1, np.s_[0:1, :], amx, np.s_[:, 0:1])
        p.op("vector", lambda e: e.tensor_reduce(
            MZ.ap[0:1, 3:4], row1.ap[0:1, :], axis=AX.X, op=OP.max),
            reads=(row1,), writes=(MZ,))
        p.dma("gpsimd", lambda e: e.dma_start(
            out=o_am.ap, in_=MZ.ap[0:1, 3:4]), reads=(MZ,), writes=(o_am,))
        p.emit()
    return nc


def launch_C_inputs(inputs, mapsA, resA, resB):
    qkv_w = np.asarray(inputs["L1_attn_qkv_w"], np.float32)
    com = dict(
        qkv_wT=np.ascontiguousarray(qkv_w.T),
        qkv_b=np.asarray(inputs["L1_attn_qkv_b"], np.float32)
        .reshape(1, 3 * D),
        gam_c=_col8(np.asarray(inputs["L1_attn_norm_w"], np.float32)),
        bet_c=_col8(np.asarray(inputs["L1_attn_norm_b"], np.float32)),
    )
    maps = []
    for core in range(8):
        b = core // 2
        mate = 2 * b + (1 - core % 2)
        m = dict(com)
        m["acc"] = np.asarray(resB[core]["o_acc"], np.float32)
        m["l_all"] = np.concatenate(
            [np.asarray(resB[core]["o_l"], np.float32),
             np.asarray(resB[mate]["o_l"], np.float32)], axis=1)
        m["mh_all"] = np.concatenate(
            [np.asarray(resB[core]["o_mh"], np.float32),
             np.asarray(resB[mate]["o_mh"], np.float32)], axis=1)
        m["g0"] = np.asarray(resA[core]["o_g0"], np.float32)
        m["xg"] = mapsA[core]["xg"]
        maps.append(m)
    return maps



def build_launch_D():
    from contextlib import ExitStack
    nc = bass.Bass("TRN2", target_bir_lowering=False)

    def in_d(nm, sh, dt=F32):
        return Buf(nc.dram_tensor(nm, sh, dt, kind="ExternalInput").ap(), nm)

    s1p_d = in_d("s1p", [2, 8, 128, D])
    am_d = in_d("am_all", [1, 8])
    v1_d = in_d("v1", [8, 128, TOK], BF16)
    xn1_d = in_d("xn1", [TOK, D])
    xn1T_d = in_d("xn1T", [D, TOK])
    x1_d = in_d("x1", [TOK, D])
    pw_d = in_d("proj_wT", [D, D])
    pb_d = in_d("proj_b", [1, D])
    fw_d = in_d("fwT", [D, D])
    fb_d = in_d("fb", [1, D])
    o_out = Buf(nc.dram_tensor("o_out", [TOK, D], F32,
                               kind="ExternalOutput").ap(), "o_out")

    with ExitStack() as st:
        c = Ctx(nc, st)
        p = c.p
        ident = make_identity(c)
        ident_f = make_identity(c, "ident_f", F32)
        to_col, to_tile, col_to_row = bcaster(c, ident_f)
        ones_col = c.sb("ones_colF", [128, 1])
        p.op("gpsimd", lambda e: e.memset(ones_col.ap[:], 1.0),
             writes=(ones_col,))

        w1sT = c.sb("w1sT", [128, 8, D], BF16)
        sm_scope = c.scope()
        sm_scope.__enter__()
        s1a = c.sb("s1a", [128, D])
        s1b2 = c.sb("s1b2", [128, D])
        rmax = c.sb("rmax", [128, 1])
        rsum = c.sb("rsum", [128, 1])
        w1sb = c.sb("w1sb", [128, D], BF16)
        for qc in range(8):
            p.dma("sync", lambda e, qc=qc: e.dma_start(
                out=s1a.ap[:], in_=s1p_d.ap[0, qc, :, :]),
                reads=(s1p_d, s1a), writes=(s1a,))
            p.dma("sync", lambda e, qc=qc: e.dma_start(
                out=s1b2.ap[:], in_=s1p_d.ap[1, qc, :, :]),
                reads=(s1p_d, s1b2), writes=(s1b2,))
            p.op("vector", lambda e: e.tensor_tensor(
                s1a.ap[:], s1a.ap[:], s1b2.ap[:], op=OP.add),
                reads=(s1a, s1b2), writes=(s1a,))
            p.op("vector", lambda e: e.tensor_reduce(
                rmax.ap[:], s1a.ap[:], axis=AX.X, op=OP.max, negate=True),
                reads=(s1a,), writes=(rmax,))
            p.op("scalar", lambda e: e.activation(
                s1a.ap[:], s1a.ap[:], AF.Exp, bias=rmax.ap[:, 0:1],
                accum_out=rsum.ap[:]), reads=(s1a, rmax),
                writes=(s1a, rsum))
            p.op("vector", lambda e: e.reciprocal(rsum.ap[:], rsum.ap[:]),
                 reads=(rsum,), writes=(rsum,))
            p.op("scalar", lambda e: e.activation(
                w1sb.ap[:], s1a.ap[:], AF.Identity, scale=rsum.ap[:, 0:1]),
                reads=(s1a, rsum), writes=(w1sb,))
            for half in range(2):
                for kk in range(4):
                    kc = half * 4 + kk
                    p.op("tensor", lambda e, kc=kc, kk=kk: e.transpose(
                        c.PT.ap[:, kk * 128:kk * 128 + 128],
                        w1sb.ap[:, kc * 128:kc * 128 + 128], ident.ap[:]),
                        reads=(w1sb, ident), writes=(c.PT,), inc=(kk == 3))
                p.op("scalar", lambda e, qc=qc, half=half: e.copy(
                    w1sT.ap[:, half * 4:half * 4 + 4,
                            qc * 128:qc * 128 + 128], c.PT.ap[:]),
                    reads=(c.PT,), writes=(w1sT,))

        sm_scope.__exit__(None, None, None)
        qwn = c.sb("qwn", [128, 8, D], F8)
        sc4 = c.sb("sc4", [1, 8])
        fwl = c.sb("fwl", [128, D])
        fsc = c.sb("fsc", [128, 9])
        tern_s = c.sb("ternD", [128, D], BF16)
        for chn in range(8):
            p.dma("sync", lambda e, chn=chn: e.dma_start(
                out=fwl.ap[:], in_=fw_d.ap.rearrange(
                    "(c p) o -> p c o", p=128)[:, chn, :]),
                reads=(fw_d, fwl), writes=(fwl,))
            p.op("vector", lambda e, chn=chn: e.tensor_reduce(
                fsc.ap[:, chn:chn + 1], fwl.ap[:], axis=AX.X,
                op=OP.add, apply_absolute_value=True),
                reads=(fwl,), writes=(fsc,))
        p.op("vector", lambda e: e.tensor_reduce(
            fsc.ap[:, 8:9], fsc.ap[:, 0:8], axis=AX.X, op=OP.add),
            reads=(fsc,), writes=(fsc,))
        p.op("tensor", lambda e: e.matmul(
            c.PR.ap[0:1, 0:1], ones_col.ap[:], fsc.ap[:, 8:9],
            start=True, stop=True), reads=(fsc, ones_col), writes=(c.PR,))
        p.op("vector", lambda e: e.tensor_scalar(
            sc4.ap[0:1, 1:2], c.PR.ap[0:1, 0:1], 1.0 / (D * D), None,
            op0=OP.mult), reads=(c.PR,), writes=(sc4,))
        halfw = c.sb("halfw", [128, 1])
        to_col(halfw, np.s_[:, 0:1], sc4, np.s_[0:1, 1:2])
        p.op("vector", lambda e: e.tensor_scalar(
            halfw.ap[:], halfw.ap[:], 0.5, None, op0=OP.mult),
            reads=(halfw,), writes=(halfw,))
        for chn in range(8):
            p.dma("sync", lambda e, chn=chn: e.dma_start(
                out=fwl.ap[:], in_=fw_d.ap.rearrange(
                    "(c p) o -> p c o", p=128)[:, chn, :]),
                reads=(fw_d, fwl), writes=(fwl,))
            p.op("vector", lambda e, chn=chn: e.tensor_scalar(
                tern_s.ap[:], fwl.ap[:], halfw.ap[:, 0:1], None,
                op0=OP.is_gt), reads=(fwl, halfw), writes=(tern_s,))
            p.op("vector", lambda e, chn=chn: e.tensor_scalar(
                fwl.ap[:], fwl.ap[:], -1.0, halfw.ap[:, 0:1],
                op0=OP.mult, op1=OP.is_gt), reads=(fwl, halfw),
                writes=(fwl,))
            p.op("vector", lambda e, chn=chn: e.tensor_tensor(
                qwn.ap[:, chn, :], tern_s.ap[:], fwl.ap[:], op=OP.subtract),
                reads=(tern_s, fwl), writes=(qwn,))
        amr = c.sb("amr", [1, 8])
        p.dma("sync", lambda e: e.dma_start(out=amr.ap[:], in_=am_d.ap),
              reads=(am_d,), writes=(amr,))
        p.op("vector", lambda e: e.tensor_reduce(
            sc4.ap[0:1, 2:3], amr.ap[0:1, :], axis=AX.X, op=OP.max),
            reads=(amr,), writes=(sc4,))
        p.op("vector", lambda e: e.reciprocal(
            sc4.ap[0:1, 3:4], sc4.ap[0:1, 2:3]), reads=(sc4,), writes=(sc4,))
        p.op("vector", lambda e: e.tensor_scalar(
            sc4.ap[0:1, 3:4], sc4.ap[0:1, 3:4], 127.0, None, op0=OP.mult),
            reads=(sc4,), writes=(sc4,))
        p.op("vector", lambda e: e.tensor_scalar(
            sc4.ap[0:1, 4:5], sc4.ap[0:1, 2:3], 1.0 / 127.0, None,
            op0=OP.mult), reads=(sc4,), writes=(sc4,))
        p.op("vector", lambda e: e.tensor_scalar(
            sc4.ap[0:1, 5:6], sc4.ap[0:1, 4:5], sc4.ap[0:1, 1:2], None,
            op0=OP.mult), reads=(sc4,), writes=(sc4,))
        qs_c = c.sb("qs_c", [128, 1])
        qr_c = c.sb("qr_c", [128, 1])
        gs_c = c.sb("gs_c", [128, 1])
        to_col(qs_c, np.s_[:, 0:1], sc4, np.s_[0:1, 3:4])
        to_col(qr_c, np.s_[:, 0:1], sc4, np.s_[0:1, 4:5])
        to_col(gs_c, np.s_[:, 0:1], sc4, np.s_[0:1, 5:6])

        pw_sb = c.sb("pw_sb", [128, 8, D], BF16)
        p.dma("gpsimd", lambda e: e.dma_start(
            out=pw_sb.ap[:], in_=pw_d.ap.rearrange("(c p) o -> p c o",
                                                   p=128)),
            reads=(pw_d,), writes=(pw_sb,))
        pbrow = c.sb("pbrow", [1, D])
        fbrow = c.sb("fbrow", [1, D])
        p.dma("sync", lambda e: e.dma_start(out=pbrow.ap[:], in_=pb_d.ap),
              reads=(pb_d,), writes=(pbrow,))
        p.dma("sync", lambda e: e.dma_start(out=fbrow.ap[:], in_=fb_d.ap),
              reads=(fb_d,), writes=(fbrow,))
        pb_bc = c.sb("pb_bc", [128, D])
        fb_bc = c.sb("fb_bc", [128, D])
        for hh in range(2):
            to_tile(pb_bc, np.s_[:, hh * 512:hh * 512 + 512], pbrow,
                    np.s_[0:1, hh * 512:hh * 512 + 512], 512)
            to_tile(fb_bc, np.s_[:, hh * 512:hh * 512 + 512], fbrow,
                    np.s_[0:1, hh * 512:hh * 512 + 512], 512)

        HW = TOK // 2
        v1_sb = c.sb("v1_sb", [128, 8, HW], BF16)
        a_fm = c.sb("a_fm", [128, 8, HW], BF16)
        qi_fm = c.sb("qi_fm", [128, 8, HW], BF16)
        xnf_l = c.sb("xnf_l", [128, HW])
        ci3 = c.sb("ci3", [128, HW], I32)
        rs3 = c.sb("rs3", [128, HW])
        xn1ch = c.sb("xn1ch", [128, D])
        x1ch = c.sb("x1ch", [128, D])
        s1ch = c.sb("s1ch", [128, D])
        g1ch = c.sb("g1ch", [128, D])
        for half in range(2):
            c0 = half * HW
            p.dma("scalar", lambda e, c0=c0: e.dma_start(
                out=v1_sb.ap[:], in_=v1_d.ap.rearrange("c p t -> p c t")
                [:, :, c0:c0 + HW]), reads=(v1_d, v1_sb), writes=(v1_sb,))
            for dc in range(8):
                p.dma("sync", lambda e, dc=dc, c0=c0: e.dma_start(
                    out=xnf_l.ap[:], in_=xn1T_d.ap.rearrange(
                        "(c p) t -> p c t", p=128)[:, dc, c0:c0 + HW]),
                    reads=(xn1T_d, xnf_l), writes=(xnf_l,))
                round_to(c, qi_fm, np.s_[:, dc, :], xnf_l, np.s_[:, :],
                         (qs_c, np.s_[:, 0:1]), (qr_c, np.s_[:, 0:1]),
                         ci3, rs3)
            for qc in range(8):
                for cs in range(4):
                    pb = c.PS[(2 * qc + cs) % 4]
                    fns = [(lambda e, qc=qc, cs=cs, kc=kc, pb=pb: e.matmul(
                        pb.ap[:], w1sT.ap[:, kc, qc * 128:qc * 128 + 128],
                        v1_sb.ap[:, kc, cs * 512:cs * 512 + 512],
                        start=(kc == 0), stop=(kc == 7)))
                        for kc in range(8)]
                    p.mm_group(fns, reads=(w1sT, v1_sb), write=pb)
                    p.op("scalar", lambda e, qc=qc, cs=cs, pb=pb: e.copy(
                        a_fm.ap[:, qc, cs * 512:cs * 512 + 512], pb.ap[:]),
                        reads=(pb,), writes=(a_fm,))
            for cc in range(16):
                gi = half * 16 + cc
                p.dma("sync", lambda e, gi=gi: e.dma_start(
                    out=xn1ch.ap[:], in_=xn1_d.ap.rearrange(
                        "(c p) d -> p c d", p=128)[:, gi, :]),
                    reads=(xn1_d, xn1ch), writes=(xn1ch,))
                p.dma("scalar", lambda e, gi=gi: e.dma_start(
                    out=x1ch.ap[:], in_=x1_d.ap.rearrange(
                        "(c p) d -> p c d", p=128)[:, gi, :]),
                    reads=(x1_d, x1ch), writes=(x1ch,))
                for hh in range(2):
                    pb = c.PS[4 + hh]
                    fns = [(lambda e, cc=cc, hh=hh, qc=qc, pb=pb: e.matmul(
                        pb.ap[:], a_fm.ap[:, qc, cc * 128:cc * 128 + 128],
                        pw_sb.ap[:, qc, hh * 512:hh * 512 + 512],
                        start=(qc == 0), stop=(qc == 7)))
                        for qc in range(8)]
                    p.mm_group(fns, reads=(a_fm, pw_sb), write=pb)
                    p.op("vector", lambda e, hh=hh, pb=pb: e.tensor_tensor(
                        s1ch.ap[:, hh * 512:hh * 512 + 512], pb.ap[:],
                        pb_bc.ap[:, hh * 512:hh * 512 + 512], op=OP.add),
                        reads=(pb, pb_bc), writes=(s1ch,))
                p.op("vector", lambda e: e.tensor_tensor(
                    s1ch.ap[:], s1ch.ap[:], xn1ch.ap[:], op=OP.add),
                    reads=(s1ch, xn1ch), writes=(s1ch,))
                for hh in range(2):
                    pb = c.PS[4 + hh]
                    fns = [(lambda e, cc=cc, hh=hh, dc=dc, pb=pb: e.matmul(
                        pb.ap[:], qi_fm.ap[:, dc, cc * 128:cc * 128 + 128],
                        qwn.ap[:, dc, hh * 512:hh * 512 + 512],
                        start=(dc == 0), stop=(dc == 7)))
                        for dc in range(8)]
                    p.mm_group(fns, reads=(qi_fm, qwn), write=pb)
                    p.op("vector", lambda e, hh=hh, pb=pb:
                         e.scalar_tensor_tensor(
                             g1ch.ap[:, hh * 512:hh * 512 + 512], pb.ap[:],
                             gs_c.ap[:, 0:1],
                             fb_bc.ap[:, hh * 512:hh * 512 + 512],
                             op0=OP.mult, op1=OP.add),
                         reads=(pb, gs_c, fb_bc), writes=(g1ch,))
                p.op("scalar", lambda e: e.activation(
                    g1ch.ap[:], g1ch.ap[:], AF.Silu), reads=(g1ch,),
                    writes=(g1ch,))
                p.op("vector", lambda e: e.tensor_tensor(
                    g1ch.ap[:], g1ch.ap[:], s1ch.ap[:], op=OP.mult),
                    reads=(g1ch, s1ch), writes=(g1ch,))
                p.op("vector", lambda e: e.tensor_tensor(
                    g1ch.ap[:], g1ch.ap[:], x1ch.ap[:], op=OP.add),
                    reads=(g1ch, x1ch), writes=(g1ch,))
                p.dma("gpsimd", lambda e, gi=gi: e.dma_start(
                    out=o_out.ap.rearrange("(c p) d -> p c d", p=128)
                    [:, gi, :], in_=g1ch.ap[:]), reads=(g1ch,),
                    writes=(o_out,))
        p.emit()
    return nc


def launch_D_inputs(inputs, mapsA, resC):
    com = dict(
        proj_wT=np.ascontiguousarray(
            np.asarray(inputs["L1_attn_proj_w"], np.float32).T),
        proj_b=np.asarray(inputs["L1_attn_proj_b"], np.float32)
        .reshape(1, D),
        fwT=np.ascontiguousarray(
            np.asarray(inputs["L1_ffn_weight"], np.float32).T),
        fb=np.asarray(inputs["L1_ffn_bias"], np.float32).reshape(1, D),
        am_all=np.array([[float(np.asarray(resC[i]["o_am"]).ravel()[0])
                          for i in range(8)]], np.float32),
    )
    maps = []
    for core in range(8):
        b = core // 2
        mate = 2 * b + (1 - core % 2)
        m = dict(com)
        m["s1p"] = np.stack([np.asarray(resC[core]["o_s1p"], np.float32),
                             np.asarray(resC[mate]["o_s1p"], np.float32)])
        m["v1"] = np.asarray(resC[core]["o_v1"])
        xn1 = np.asarray(resC[core]["o_xn1"], np.float32)
        m["xn1"] = xn1
        m["xn1T"] = np.ascontiguousarray(xn1.T)
        m["x1"] = np.asarray(resC[core]["o_x1"], np.float32)
        maps.append(m)
    return maps



def kernel(**inputs):
    mapsA = launch_A_inputs(inputs)
    ncA = build_launch_A()
    resA = run_bass_kernel_spmd(ncA, mapsA, core_ids=list(range(8))).results
    mapsB = launch_B_inputs(inputs, resA)
    ncB = build_launch_B()
    resB = run_bass_kernel_spmd(ncB, mapsB, core_ids=list(range(8))).results
    mapsC = launch_C_inputs(inputs, mapsA, resA, resB)
    ncC = build_launch_C()
    resC = run_bass_kernel_spmd(ncC, mapsC, core_ids=list(range(8))).results
    mapsD = launch_D_inputs(inputs, mapsA, resC)
    ncD = build_launch_D()
    resD = run_bass_kernel_spmd(ncD, mapsD, core_ids=list(range(8))).results
    out = np.empty((B, T, D), np.float32)
    for b in range(B):
        out[b] = unshard_rows(np.asarray(resD[2 * b]["o_out"], np.float32),
                              np.asarray(resD[2 * b + 1]["o_out"],
                                         np.float32))
    return out
